# version 73 (final)
# baseline (speedup 1.0000x reference)
import contextlib
import numpy as np
import ml_dtypes

import concourse.bass as bass
import concourse.mybir as mybir
from concourse.bass_utils import run_bass_kernel_spmd
from concourse.alu_op_type import AluOpType as ALU

F32 = mybir.dt.float32
BF16 = mybir.dt.bfloat16
AF = mybir.ActivationFunctionType
BF = ml_dtypes.bfloat16

P = 128
C = 1024
NT = 2048
HALO = 512
QCW = 512
NQC = NT // QCW
H = 16
D = 64
FF = 4096
TC = 256
NCH = NT // TC
NEG = -1e30

UW = [128, 256, 384, 512, 512, 384, 256, 128]
ULO = [0, 0, 0, 0, 0, 128, 256, 384]
GOFF = [896, 1024, 512, 0, 0, 512, 1024, 896]
GRP = [0, 0, 0, 0, 1, 1, 1, 1]
RUNLO = [0, 128, 256, 384, 0, 128, 256, 384]

CONSTS = [1e-5, -0.5, 0.125, 1.0, 0.0, -1.0]

ENG_SEM = {"sync": "qS", "gpsimd": "qG", "tensor": "tT", "vector": "vV", "scalar": "sS"}
DMA_SEMS = {"qS", "qG", "qX", "qC", "qR"}

oWQ, oWK, oWV, oWO = 0, 8192, 16384, 24576
oASTG = 32768
oAINT = 34816
oQT = 43008
oKT = 47104
oVX = 55296
oPT = 63616
oAN = 71296
oAT = 75392
oW1 = 0
oW1B = 34816
oW2A = 43008
oW2B = 55296
oW2C = 24576
oFT = 30720
oFST = 32768
oHS = 69632
NB = 77824


def w1col(db):
    return db * 4096 if db < 6 else oW1B + (db - 6) * 4096


def w2col(fb):
    if fb < 12:
        return oW2A + fb * 1024
    if fb < 26:
        return oW2B + (fb - 12) * 1024
    return oW2C + (fb - 26) * 1024

oX0 = 0
oX1S = 8192
oX1R = 0
oYT = 6144
oYT1 = 7168
oX2S = 8192
oGB = 10240
oBB = 11264
NF = 12288


class Res:

    __slots__ = ("w", "r")

    def __init__(self):
        self.w = None
        self.r = {}

    def reads(self):
        return list(self.r.items())

    def note_read(self, ev):
        if ev is not None:
            s, v = ev
            if self.r.get(s, 0) < v:
                self.r[s] = v

    def note_write(self, ev):
        self.w = ev
        self.r = {}


class Sched:
    def __init__(self):
        self.streams = {e: [] for e in ENG_SEM}
        self.counts = {s: 0 for s in list(ENG_SEM.values()) + ["qX", "qC", "qR", "gG"]}
        self.lastwait = {}
        self.R = {}
        self.pending = {e: [] for e in ENG_SEM}
        self.marks = {}

    def mark(self, name):
        self.marks[name] = dict(self.counts)

    def res(self, name):
        if name not in self.R:
            self.R[name] = Res()
        return self.R[name]

    def haz(self, res_list):
        evs = []
        for r in res_list:
            if r.w is not None:
                evs.append(r.w)
            evs.extend(r.reads())
        return evs

    def op(self, eng, emit, waits=(), reads=(), writes=(), inc=True, sem_name=None):
        st = self.streams[eng]
        evs = list(waits)
        for r in reads:
            if r.w is not None:
                evs.append(r.w)
        for r in writes:
            if r.w is not None:
                evs.append(r.w)
            evs.extend(r.reads())
        wl = []
        for ev in evs:
            if ev is None:
                continue
            s, v = ev
            if eng == "tensor" and s == "tT":
                continue
            k = (eng, s)
            if v > self.lastwait.get(k, 0):
                wl.append((s, v))
                self.lastwait[k] = v
        sem = sem_name or ENG_SEM[eng]
        by = 16 if sem in DMA_SEMS else 1
        ev = None
        if inc:
            self.counts[sem] += by
            ev = (sem, self.counts[sem])
        st.append((wl, emit, sem if inc else None, by))
        if ev is None:
            self.pending[eng].append((list(reads), list(writes)))
        else:
            for rds, wrs in self.pending[eng]:
                for r in rds:
                    r.note_read(ev)
                for r in wrs:
                    r.note_write(ev)
            self.pending[eng] = []
            for r in reads:
                r.note_read(ev)
            for r in writes:
                r.note_write(ev)
        return ev


def schedule(nc, sch, dr, sb, nt=NT):
    nqc = nt // QCW
    nch = nt // TC
    AB, AFa = sb["AB"], sb["AF"]
    PSGA, PSGB, PSM = sb["PSGA"], sb["PSGB"], sb["PSM"]
    PSMbf = PSM[:].bitcast(BF16)
    PSGAbf = PSGA[:].bitcast(BF16)
    PSGBbf = PSGB[:].bitcast(BF16)
    identb = sb["identb"][:]
    op, res, haz = sch.op, sch.res, sch.haz

    def ab(o, n):
        return AB[:, o : o + n]

    def af(o, n):
        return AFa[:, o : o + n]

    def dma(eng, dst, src, deps=(), writes=(), reads=(), sem_name=None):
        sem = sem_name or ENG_SEM[eng]
        chain = [(sem, sch.counts[sem])] if sch.counts[sem] else []
        return op(eng, lambda e, d=dst, s=src: e.dma_start(out=d, in_=s),
                  waits=list(deps) + chain, writes=list(writes), reads=list(reads),
                  sem_name=sem_name)

    PSA = [res(f"psga{i}") for i in range(3)]
    PSB = [res(f"psgb{i}") for i in range(3)]
    PSMB = [res(f"psm{i}") for i in range(2)]
    PSMH = [[PSMB[0]], [PSMB[1]]]

    rcst = res("cst")
    rident = res("ident")

    def apply_op(aeng, out_ap, in_ap, rstd, nmr, reads, writes):
        if aeng == "scalar":
            op("scalar", lambda e, x=in_ap, r=rstd, n=nmr, o=out_ap:
               e.activation(o, x, AF.Identity, bias=n, scale=r),
               reads=reads, writes=writes)
        else:
            op("vector", lambda e, x=in_ap, r=rstd, n=nmr, o=out_ap:
               e.tensor_scalar(o, x, r, n, ALU.mult, ALU.add),
               reads=reads, writes=writes)

    def ln_stats_phased(specs, aeng="vector"):
        outs = []
        hs = []
        for x_ap, slot, psum_reads, extra_reads in specs:
            st = sb["stat"][:, slot * 16 : slot * 16 + 12]
            mv = sb["stat"][:, slot * 16 + 12 : slot * 16 + 14]
            rstd = sb["rn"][:, slot * 2 : slot * 2 + 1]
            nmr = sb["rn"][:, slot * 2 + 1 : slot * 2 + 2]
            rs, rm, rr = res(f"st{slot}"), res(f"mv{slot}"), res(f"rn{slot}")
            hs.append((x_ap, st, mv, rstd, nmr, rs, rm, rr, psum_reads, extra_reads))
            outs.append((rstd, nmr, rr))
        for x_ap, st, mv, rstd, nmr, rs, rm, rr, pr, er in hs:
            if isinstance(x_ap, tuple):
                a0, a1 = x_ap
            else:
                a0, a1 = x_ap[:, 0:512], x_ap[:, 512:1024]
            op("vector", lambda e, a=a0, o=st: e.bn_stats(o[:, 0:6], a),
               reads=list(pr) + list(er), writes=[rs], inc=False)
            op("vector", lambda e, a=a1, o=st: e.bn_stats(o[:, 6:12], a),
               reads=list(pr) + list(er), writes=[rs])
        for x_ap, st, mv, rstd, nmr, rs, rm, rr, pr, er in hs:
            op("vector", lambda e, i=st, o=mv: e.bn_aggr(o, i), reads=[rs], writes=[rm])
        for x_ap, st, mv, rstd, nmr, rs, rm, rr, pr, er in hs:
            op("scalar", lambda e, i=mv, o=rstd: e.activation(o, i[:, 1:2], AF.Ln, bias=1e-5, scale=1.0),
               reads=[rm, rcst], writes=[rr])
        for x_ap, st, mv, rstd, nmr, rs, rm, rr, pr, er in hs:
            op("scalar", lambda e, o=rstd: e.activation(o, o, AF.Exp, bias=0.0, scale=-0.5),
               reads=[rr, rcst], writes=[rr])
        for x_ap, st, mv, rstd, nmr, rs, rm, rr, pr, er in hs:
            if aeng == "scalar":
                op("scalar", lambda e, i=mv, r=rstd, o=nmr:
                   e.activation(o, i[:, 0:1], AF.Identity, bias=0.0, scale=r),
                   reads=[rm, rr, rcst], writes=[rr])
                op("scalar", lambda e, o=nmr: e.activation(o, o, AF.Identity, bias=0.0, scale=-1.0),
                   reads=[rr, rcst], writes=[rr])
            else:
                op("vector", lambda e, i=mv, r=rstd, o=nmr:
                   e.tensor_scalar(o, i[:, 0:1], r, -1.0, ALU.mult, ALU.mult),
                   reads=[rm, rr, rcst], writes=[rr])
        return outs

    def ln_stats(x_ap, slot, psum_reads=(), extra_reads=(), aeng="vector"):
        st = sb["stat"][:, slot * 16 : slot * 16 + 12]
        mv = sb["stat"][:, slot * 16 + 12 : slot * 16 + 14]
        rstd = sb["rn"][:, slot * 2 : slot * 2 + 1]
        nmr = sb["rn"][:, slot * 2 + 1 : slot * 2 + 2]
        rs, rm, rr = res(f"st{slot}"), res(f"mv{slot}"), res(f"rn{slot}")
        op("vector", lambda e, a=x_ap, o=st: e.bn_stats(o[:, 0:6], a[:, 0:512]),
           reads=list(psum_reads) + list(extra_reads), writes=[rs], inc=False)
        op("vector", lambda e, a=x_ap, o=st: e.bn_stats(o[:, 6:12], a[:, 512:1024]),
           reads=list(psum_reads) + list(extra_reads), writes=[rs])
        op("vector", lambda e, i=st, o=mv: e.bn_aggr(o, i), reads=[rs], writes=[rm])
        op("scalar", lambda e, i=mv, o=rstd: e.activation(o, i[:, 1:2], AF.Ln, bias=1e-5, scale=1.0),
           reads=[rm, rcst], writes=[rr])
        op("scalar", lambda e, o=rstd: e.activation(o, o, AF.Exp, bias=0.0, scale=-0.5),
           reads=[rr, rcst], writes=[rr])
        if aeng == "scalar":
            op("scalar", lambda e, i=mv, r=rstd, o=nmr:
               e.activation(o, i[:, 0:1], AF.Identity, bias=0.0, scale=r),
               reads=[rm, rr, rcst], writes=[rr])
            op("scalar", lambda e, o=nmr: e.activation(o, o, AF.Identity, bias=0.0, scale=-1.0),
               reads=[rr, rcst], writes=[rr])
        else:
            op("vector", lambda e, i=mv, r=rstd, o=nmr:
               e.tensor_scalar(o, i[:, 0:1], r, -1.0, ALU.mult, ALU.mult),
               reads=[rm, rr, rcst], writes=[rr])
        return rstd, nmr, rr

    def _setup_consts(which):
        if which == 0:
            dma("sync", sb["cst"][:], dr["cst"], writes=[rcst], sem_name="qC")
            dma("sync", sb["identb"][:], dr["identb"], writes=[rident], sem_name="qC")
        else:
            dma("sync", sb["mm"][:], dr["mm"], writes=[res("mm")], sem_name="qC")
            dma("sync", sb["mh"][:], dr["mh"], writes=[res("mh")], sem_name="qC")
            dma("sync", sb["b1r"][:], dr["b1r"], writes=[res("b1r")], sem_name="qC")

    def x0_dma(qc):
        grp = qc % 2
        if qc < 0:
            rows = dr["xs"][0:HALO, :]
        else:
            rows = dr["xs"][HALO + qc * QCW : HALO + (qc + 1) * QCW, :]
        return dma("sync", af(oX0 + grp * 4096, 4096).rearrange("p (i c) -> p i c", c=C),
                   rows.rearrange("(i p) c -> p i c", p=P),
                   writes=[res(f"x0g{grp}")], sem_name="qX")

    def x0_dma_half(qc, hf):
        grp = qc % 2
        r0 = 0 if qc < 0 else HALO + qc * QCW
        rows = dr["xs"][r0 + hf * 2 * P : r0 + (hf + 1) * 2 * P, :]
        rh = res(f"x0h{qc}_{hf}")
        dma("sync", af(oX0 + grp * 4096 + hf * 2048, 2048).rearrange("p (i c) -> p i c", c=C),
            rows.rearrange("(i p) c -> p i c", p=P),
            writes=[rh, res(f"x0g{grp}")], sem_name="qX")
        return rh

    rxh = {(-1, 0): x0_dma_half(-1, 0)}
    _setup_consts(0)
    rxh[(-1, 1)] = x0_dma_half(-1, 1)
    rxh[(0, 0)] = x0_dma_half(0, 0)
    rxh[(0, 1)] = x0_dma_half(0, 1)
    _setup_consts(1)
    wdeps = [("qX", sch.counts["qX"]), ("qC", sch.counts["qC"])]
    for nm, o in (("wkt", oWK), ("wqt", oWQ), ("wvt", oWV), ("wot", oWO)):
        dma("gpsimd", ab(o, 8192).rearrange("p (cb c) -> p cb c", c=C),
            dr[nm].rearrange("(cb p) c -> p cb c", p=P), writes=[res(nm)],
            deps=wdeps)
        wdeps = []

    ln1_stats_of = {}

    def ln1_stats(qc, tiles=(0, 1, 2, 3), rx=None):
        grp = qc % 2
        if rx is None:
            rx = res(f"x0g{grp}")
        specs = []
        for i in tiles:
            slotn = 4 * qc + i
            x_ap = af(oX0 + grp * 4096 + i * 1024, 1024)
            specs.append((x_ap, slotn % 8, (), [rx]))
        ln1_stats_of.setdefault(qc, {})
        for i, st in zip(tiles, ln_stats_phased(specs)):
            ln1_stats_of[qc][i] = st

    def ln1_apply(qc, tiles):
        grp = qc % 2
        rx = res(f"x0g{grp}")
        stats = ln1_stats_of[qc]
        for i in tiles:
            slotn = 4 * qc + i
            x_ap = af(oX0 + grp * 4096 + i * 1024, 1024)
            rstd, nmr, rr = stats[i]
            pg = slotn % 2
            stg = ab(oASTG + pg * 1024, 1024)
            rstg = res(f"astg{pg}")
            apply_op("vector", stg, x_ap, rstd, nmr, [rx, rr, rcst], [rstg])

    def ln1_rest(qc, tiles=(0, 1, 2, 3), applied=False):
        grp = qc % 2
        rx = res(f"x0g{grp}")
        stats = ln1_stats_of[qc]
        for i in tiles:
            slotn = 4 * qc + i
            tok = qc * QCW + i * P
            slot = (tok % 1024) // P
            if not applied:
                ln1_apply(qc, (i,))
            pg = slotn % 2
            stg = ab(oASTG + pg * 1024, 1024)
            rstg = res(f"astg{pg}")
            for cb in range(8):
                PSbf = PSGAbf if cb < 4 else PSGBbf
                rpb = PSA[2] if cb < 4 else PSB[2]
                op("tensor",
                   lambda e, cb=cb, stg=stg, PSbf=PSbf:
                   e.transpose(PSbf[:, 2560 + (cb % 4) * 128 : 2560 + (cb % 4 + 1) * 128],
                               stg[:, cb * 128 : (cb + 1) * 128], identb),
                   reads=[rstg, rident], writes=[rpb], inc=(cb % 4 == 3))
            rslot = res(f"aint{slot}")
            for cb in range(8):
                PSbf = PSGAbf if cb < 4 else PSGBbf
                rpb = PSA[2] if cb < 4 else PSB[2]
                op("vector",
                   lambda e, cb=cb, slot=slot, PSbf=PSbf:
                   e.tensor_copy(ab(oAINT + cb * 1024 + slot * P, P),
                                 PSbf[:, 2560 + (cb % 4) * 128 : 2560 + (cb % 4 + 1) * 128]),
                   reads=[rpb], writes=[rslot], inc=(cb % 4 == 3))

    def proj_chunk(qc, what, halo=False):
        tok0 = -HALO if halo else qc * QCW
        s0 = (tok0 % 1024) // P
        acol = tok0 % 1024
        aint_reads = [res(f"aint{(s0 + k) % 8}") for k in range(4)]
        if what in ("q", "k"):
            woff, rw = (oWQ, res("wqt")) if what == "q" else (oWK, res("wkt"))
            for co in range(8):
                bank = co % 2
                rpb = PSA[bank]
                for cb in range(8):
                    op("tensor",
                       lambda e, co=co, cb=cb, bank=bank, woff=woff, acol=acol:
                       e.matmul(PSGA[:, bank * 512 : bank * 512 + 512],
                                ab(woff + cb * 1024 + co * 128, 128),
                                ab(oAINT + cb * 1024 + acol, 512),
                                start=(cb == 0), stop=(cb == 7)),
                       reads=[rw] + aint_reads, writes=[rpb], inc=(cb == 7))
                if what == "q":
                    dst, rdst = ab(oQT + co * 512, 512), res("qT")
                else:
                    dst, rdst = ab(oKT + co * 1024 + acol, 512), res(f"kts{s0 // 4}")
                op("vector",
                   lambda e, dst=dst, bank=bank:
                   e.tensor_copy(dst, PSGA[:, bank * 512 : bank * 512 + 512]),
                   reads=[rpb], writes=[rdst])
        else:
            rw = res("wvt")
            for i in range(4):
                slot = (s0 + i) % 8
                for ch in range(2):
                    rpb = PSA[ch]
                    for cb in range(8):
                        op("tensor",
                           lambda e, i=i, ch=ch, cb=cb, acol=acol:
                           e.matmul(PSGA[:, ch * 512 : ch * 512 + 512],
                                    ab(oAINT + cb * 1024 + acol + i * P, P),
                                    ab(oWV + cb * 1024 + ch * 512, 512),
                                    start=(cb == 0), stop=(cb == 7)),
                           reads=[rw] + aint_reads, writes=[rpb], inc=(cb == 7))
                    rvx = res(f"vx{slot}")
                    op("vector",
                       lambda e, slot=slot, ch=ch:
                       e.tensor_copy(
                           ab(oVX + slot * 1040 + ch * 520, 520).rearrange(
                               "p (h x) -> p h x", x=65)[:, :, 0:64],
                           PSGA[:, ch * 512 : ch * 512 + 512].rearrange(
                               "p (h x) -> p h x", x=64)),
                       reads=[rpb], writes=[rvx])

    def attn_st_pair(qc, h0, hx0, g):
        qc0 = qc * QCW
        rkts = [res("kts0"), res("kts1")]
        rqt = res("qT")
        heads = [(h0, hx0 % 3, PSGA, PSA), (h0 + 1, (hx0 + 1) % 3, PSGB, PSB)]
        subs = [(3, 2), (0, 1)] if g == 0 else [(4, 5), (7, 6)]
        for si, sub in enumerate(subs):
            for j in sub:
                kslotc = (qc0 - 512 + 128 * j) % 1024
                for h, a, PSG, banks in heads:
                    ht, hb = h // 2, (h % 2) * 64
                    op("tensor",
                       lambda e, j=j, PSG=PSG, kslotc=kslotc, ht=ht, hb=hb:
                       e.matmul(PSG[:, GOFF[j] : GOFF[j] + UW[j]],
                                ab(oKT + ht * 1024 + kslotc, 128)[hb : hb + 64, :],
                                ab(oQT + ht * 512 + ULO[j], UW[j])[hb : hb + 64, :],
                                start=True, stop=False),
                       reads=rkts + [rqt], writes=[banks[GOFF[j] // 512]], inc=False)
            for j in sub:
                kb0 = qc0 - 512 + 128 * j
                if kb0 < 0:
                    m_ap, rmask = sb["mh"][:, 384 - 128 * j : 384 - 128 * j + UW[j]], res("mh")
                    pcol, mw = GOFF[j], UW[j]
                else:
                    lo = 128 if j <= 3 else 0
                    m_ap, rmask = sb["mm"][:, lo : lo + 128], res("mm")
                    pcol, mw = GOFF[j] + RUNLO[j] - ULO[j], 128
                for h, a, PSG, banks in heads:
                    op("tensor",
                       lambda e, PSG=PSG, pcol=pcol, m=m_ap, mw=mw:
                       e.matmul(PSG[:, pcol : pcol + mw], identb, m, start=False, stop=True),
                       reads=[rmask, rident], writes=[banks[GOFF[j] // 512]],
                       inc=(si == 1 and j == sub[1]))
        for h, a, PSG, banks in heads:
            op("scalar",
               lambda e, PSG=PSG, a=a, g=g:
               e.activation(ab(oPT + a * 2560 + g * 1280, 1280), PSG[:, 0:1280],
                            AF.Exp, bias=0.0, scale=0.125),
               reads=banks + [rcst], writes=[res(f"pt{a}g{g}")])

    def attn_pv(qc, h, hidx):
        qc0 = qc * QCW
        a = hidx % 3
        for i in range(4):
            pv = i % 2
            rpv = PSMB[pv]
            for jj, j in enumerate(range(i, i + 5)):
                g = GRP[j]
                ptcol = a * 2560 + g * 1280 + GOFF[j] + 128 * i - ULO[j]
                slot = ((qc0 - 512 + 128 * j) % 1024) // 128
                op("tensor",
                   lambda e, ptcol=ptcol, slot=slot, h=h, pv=pv, jj=jj:
                   e.matmul(PSM[:, pv * 512 : pv * 512 + 65],
                            ab(oPT + ptcol, 128),
                            ab(oVX + slot * 1040 + h * 65, 65),
                            start=(jj == 0), stop=(jj == 4)),
                   reads=[res(f"pt{a}g{g}"), res(f"vx{slot}")],
                   writes=[rpv], inc=(jj == 4))
            rrc = res(f"rcp{pv}")
            op("vector",
               lambda e, pv=pv:
               e.reciprocal(sb["rcp"][:, pv : pv + 1], PSM[:, pv * 512 + 64 : pv * 512 + 65]),
               reads=[rpv], writes=[rrc])
            op("vector",
               lambda e, pv=pv, i=i, h=h:
               e.tensor_scalar(ab(oAN + i * 1024 + h * 64, 64),
                               PSM[:, pv * 512 : pv * 512 + 64],
                               sb["rcp"][:, pv : pv + 1], None, ALU.mult),
               reads=[rpv, rrc, rcst], writes=[res(f"an{i}")])

    def wo_uses_psm(i):
        return i % 4 in (0, 3)

    def wo_out(i, ch):
        if not wo_uses_psm(i):
            return PSGB[:, ch * 512 : ch * 512 + 512], [PSB[ch]]
        return PSM[:, ch * 512 : ch * 512 + 512], PSMH[ch]

    def wo_mm(qc, i):
        even = i % 2 == 0
        PSTbf = PSGAbf if even else PSGBbf
        rtb = PSA[2] if even else PSB[2]
        for cb in range(8):
            op("tensor",
               lambda e, i=i, cb=cb, PSTbf=PSTbf:
               e.transpose(PSTbf[:, 2048 + cb * 128 : 2048 + (cb + 1) * 128],
                           ab(oAN + i * 1024 + cb * 128, 128), identb),
               reads=[res(f"an{i}"), rident], writes=[rtb], inc=(cb == 7))
        rat = res(f"at{i % 2}")
        op("scalar",
           lambda e, PSTbf=PSTbf, i=i:
           e.copy(ab(oAT + (i % 2) * 1024, 1024), PSTbf[:, 2048 : 3072]),
           reads=[rtb], writes=[rat])
        rw = res("wot")
        for ch in range(2):
            o_ap, o_res = wo_out(i, ch)
            for cb in range(8):
                op("tensor",
                   lambda e, i=i, ch=ch, cb=cb, o=o_ap:
                   e.matmul(o,
                            ab(oAT + (i % 2) * 1024 + cb * 128, 128),
                            ab(oWO + cb * 1024 + ch * 512, 512),
                            start=(cb == 0), stop=(cb == 7)),
                   reads=[rat, rw], writes=o_res, inc=(cb == 7))

    wo_tail_ctx = {}

    def wo_tail_stats(qc, i0):
        qc0 = qc * QCW
        specs = []
        pss = []
        order = (i0, i0 + 1) if not wo_uses_psm(i0) else (i0 + 1, i0)
        for i in order:
            use_b = not wo_uses_psm(i)
            PS = PSGB if use_b else PSM
            rb = [PSB[0], PSB[1]] if use_b else PSMB
            pss.append((i, PS, rb))
            specs.append((PS[:, 0:1024], (4 * qc + i) % 8, rb, ()))
        wo_tail_ctx[(qc, i0)] = (pss, ln_stats_phased(specs))

    def wo_tail_rest(qc, i0):
        qc0 = qc * QCW
        grp = qc % 2
        pss, stats = wo_tail_ctx[(qc, i0)]
        for (i, PS, rb), (rstd, nmr, rr) in zip(pss, stats):
            slotn = 4 * qc + i
            x_ap = af(oX0 + grp * 4096 + i * 1024, 1024)
            pg = slotn % 4
            x1stg = af(oX1S + pg * 1024, 1024)
            rx1 = res(f"x1st{pg}")
            apply_op("vector", x1stg, PS[:, 0:1024], rstd, nmr, rb + [rr], [rx1])
            op("vector",
               lambda e, x=x_ap, o=x1stg: e.tensor_tensor(o, o, x, ALU.add),
               reads=[res(f"x0g{grp}")], writes=[rx1])
            dma("sync", dr["x1s"][qc0 + i * P : qc0 + (i + 1) * P, :], x1stg,
                reads=[rx1], writes=[res(f"x1rows{slotn}")])

    ffn_sls = {}

    reloaded = set()

    def ffn_reload(k):
        if k in reloaded:
            return
        reloaded.add(k)
        kp = k % 3
        dma("sync", af(oX1R + kp * 2048, 2048).rearrange("p (i c) -> p i c", c=C),
            dr["x1s"][k * TC : (k + 1) * TC, :].rearrange("(i p) c -> p i c", p=P),
            deps=[sch.R[f"x1rows{2 * k}"].w, sch.R[f"x1rows{2 * k + 1}"].w],
            writes=[res(f"x1r{kp}")], sem_name="qR")

    prep_stats = {}

    def ffn_prep_stats(k):
        if k in prep_stats:
            return
        kp = k % 3
        rx1r = res(f"x1r{kp}")
        ffn_reload(k)
        sls = ffn_sls.setdefault(k, [])
        specs = []
        for u in range(2):
            x_ap = af(oX1R + kp * 2048 + u * 1024, 1024)
            sl = (2 * k + u) % 4
            sls.append(sl)
            specs.append((x_ap, sl, (), [rx1r]))
        prep_stats[k] = ln_stats_phased(specs, aeng="scalar")

    def ffn_prep(k):
        ffn_prep_stats(k)
        stats = prep_stats[k]
        kp = k % 3
        rx1r = res(f"x1r{kp}")
        for u in range(2):
            x_ap = af(oX1R + kp * 2048 + u * 1024, 1024)
            rstd, nmr, rr = stats[u]
            pg = u
            stg = ab(oFST + pg * 1024, 1024)
            rstg = res(f"fstg{pg}")
            apply_op("scalar", stg, x_ap, rstd, nmr, [rx1r, rr, rcst], [rstg])

    def ffn_prep_trans(k):
        for u in range(2):
            pg = u
            stg = ab(oFST + pg * 1024, 1024)
            rstg = res(f"fstg{pg}")
            PSbf = PSGBbf if u == 0 else PSGAbf
            rtb = PSB[2] if u == 0 else PSA[2]
            for cb in range(8):
                op("tensor",
                   lambda e, cb=cb, stg=stg, PSbf=PSbf:
                   e.transpose(PSbf[:, 2048 + cb * 128 : 2048 + (cb + 1) * 128],
                               stg[:, cb * 128 : (cb + 1) * 128], identb),
                   reads=[rstg, rident], writes=[rtb], inc=(cb == 7))
            rft = res("ft")
            for cb in range(8):
                op("vector",
                   lambda e, cb=cb, u=u, PSbf=PSbf:
                   e.tensor_copy(ab(oFT + cb * 256 + u * 128, 128),
                                 PSbf[:, 2048 + cb * 128 : 2048 + (cb + 1) * 128]),
                   reads=[rtb], writes=[rft], inc=(cb == 7))

    def ffn_w1(k):
        rft = res("ft")
        slot_order = [0, 2, 4, 10, 6, 8, 1, 3, 5, 11, 7, 9]
        for fb in range(32):
            slot = slot_order[fb % 12]
            if slot < 6:
                ps_ap = PSGA[:, slot * 256 : slot * 256 + 256]
                rpb = PSA[slot // 2]
            else:
                ps_ap = PSGB[:, (slot - 6) * 256 : (slot - 6) * 256 + 256]
                rpb = PSB[(slot - 6) // 2]
            for db in range(8):
                op("tensor",
                   lambda e, fb=fb, db=db, ps_ap=ps_ap:
                   e.matmul(ps_ap,
                            ab(w1col(db) + fb * 128, 128),
                            ab(oFT + db * 256, 256),
                            start=(db == 0), stop=(db == 7)),
                   reads=[res("w1sba") if db < 6 else res("w1sbb"), rft], writes=[rpb], inc=(db == 7))
            op("vector",
               lambda e, fb=fb, ps_ap=ps_ap:
               e.tensor_scalar(ab(oHS + fb * 256, 256), ps_ap,
                               sb["b1r"][:, fb : fb + 1], 0.0, ALU.add, ALU.max),
               reads=[rpb, res("b1r"), rcst], writes=[res("HS")])
    def ffn_w2(k, u):
        PS = PSM if u == 0 else PSGB
        rb = PSMH if u == 0 else [[PSB[0]], [PSB[1]]]
        for ch in range(2):
            for fb in range(32):
                op("tensor",
                   lambda e, u=u, ch=ch, fb=fb, PS=PS:
                   e.matmul(PS[:, ch * 512 : ch * 512 + 512],
                            ab(oHS + fb * 256 + u * 128, 128),
                            ab(w2col(fb) + ch * 512, 512),
                            start=(fb == 0), stop=False),
                   reads=[res("HS"), res("w2sba") if fb < 26 else res("w2sbb")], writes=rb[ch], inc=False)
            op("tensor",
               lambda e, ch=ch, PS=PS:
               e.matmul(PS[:, ch * 512 : ch * 512 + 512], sb["onesb"][:],
                        sb["b2rb"][0:1, ch * 512 : ch * 512 + 512],
                        start=False, stop=True),
               reads=[res("b2s"), res("ones")], writes=rb[ch], inc=True)

    def ffn_epi_eo(k, part):
        kp = k % 3
        rx1r = res(f"x1r{kp}")
        sls = ffn_sls[k]
        yts = [af(oYT, 1024), af(oYT1, 1024)]
        rys = [res("yt0"), res("yt1")]
        if part == 0:
            for u in range(2):
                sl = sls[u]
                rstd3 = sb["rn"][:, sl * 2 : sl * 2 + 1]
                nmr3 = sb["rn"][:, sl * 2 + 1 : sl * 2 + 2]
                x_ap = af(oX1R + kp * 2048 + u * 1024, 1024)
                apply_op("scalar", yts[u], x_ap, rstd3, nmr3,
                         [rx1r, res(f"rn{sl}"), rcst], [rys[u]])
            op("vector", lambda e, o=yts[0]: e.tensor_tensor(o, o, PSM[:, 0:1024], ALU.add),
               reads=[PSMB[0], PSMB[1], rys[0]], writes=[rys[0]])
        else:
            op("vector", lambda e, o=yts[1]: e.tensor_tensor(o, o, PSGB[:, 0:1024], ALU.add),
               reads=[PSB[0], PSB[1], rys[1]], writes=[rys[1]])

    def ffn_epi_rest(k, us=(0, 1)):
        kp = k % 3
        rx1r = res(f"x1r{kp}")
        yts = [af(oYT, 1024), af(oYT1, 1024)]
        zts = [af(oX2S, 1024), af(oX2S + 1024, 1024)]
        rys = [res("yt0"), res("yt1")]
        rzs = [res("x2st0"), res("x2st1")]
        st4l = ln_stats_phased([(yts[u], 4 + u, (), [rys[u]]) for u in us],
                               aeng="scalar")
        st4 = dict(zip(us, st4l))
        for u in us:
            rstd4, nmr4, rr4 = st4[u]
            apply_op("scalar", zts[u], yts[u], rstd4, nmr4, [rys[u], rr4, rcst], [rzs[u]])
        for u in us:
            op("vector", lambda e, o=zts[u]: e.tensor_tensor(o, o, af(oGB, 1024), ALU.mult),
               reads=[res("gbs")], writes=[rzs[u]])
        for u in us:
            op("vector", lambda e, o=zts[u]: e.tensor_tensor(o, o, af(oBB, 1024), ALU.add),
               reads=[res("bbs")], writes=[rzs[u]])
        st5l = ln_stats_phased([(zts[u], 6 + u, (), [rzs[u]]) for u in us],
                               aeng="scalar")
        st5 = dict(zip(us, st5l))
        for u in us:
            x_ap = af(oX1R + kp * 2048 + u * 1024, 1024)
            rstd5, nmr5, rr5 = st5[u]
            x2stg = zts[u]
            rx2 = rzs[u]
            apply_op("scalar", x2stg, x2stg, rstd5, nmr5, [rr5, rcst], [rx2])
            if k == nch - 1:
                op("vector", lambda e, x=x_ap, o=x2stg: e.tensor_tensor(o, o, x, ALU.add),
                   reads=[rx1r], writes=[rx2])
            else:
                op("gpsimd", lambda e, x=x_ap, o=x2stg: e.tensor_tensor(o, o, x, ALU.add),
                   reads=[rx1r], writes=[rx2], sem_name="gG")
            dma("sync", dr["y"][k * TC + u * P : k * TC + (u + 1) * P, :], x2stg,
                reads=[rx2], writes=[res("yout")])


    hidx = 0
    def ln1_chunk(qc):
        ln1_stats(qc)
        ln1_rest(qc)

    ln1_stats(-1, (0, 1), rx=rxh[(-1, 0)])
    ln1_stats(-1, (2, 3), rx=rxh[(-1, 1)])
    ln1_stats(0, (0, 1), rx=rxh[(0, 0)])
    ln1_stats(0, (2, 3), rx=rxh[(0, 1)])
    ln1_apply(-1, (0, 1))
    ln1_rest(-1, (0,), applied=True)
    ln1_apply(-1, (2,))
    ln1_rest(-1, (1,), applied=True)
    ln1_apply(-1, (3,))
    ln1_rest(-1, (2, 3), applied=True)
    x0_dma(1)
    ln1_apply(0, (0, 1))
    ln1_rest(0, (0,), applied=True)
    ln1_apply(0, (2,))
    ln1_rest(0, (1,), applied=True)
    ln1_apply(0, (3,))
    ln1_rest(0, (2, 3), applied=True)
    for slot in range(8):
        op("vector",
           lambda e, slot=slot: e.memset(
               ab(oVX + slot * 1040, 1040).rearrange("p (h x) -> p h x", x=65)[:, :, 64:65], 1.0),
           writes=[res(f"vx{slot}")])
    proj_chunk(0, "k", halo=True)
    proj_chunk(0, "q")
    attn_st_pair(0, 0, 0, 0)
    proj_chunk(0, "k")
    attn_st_pair(0, 0, 0, 1)
    proj_chunk(0, "v", halo=True)
    proj_chunk(0, "v")
    for qc in range(nqc):
        for pr in range(H // 2):
            if not (qc == 0 and pr == 0):
                attn_st_pair(qc, 2 * pr, hidx + 2 * pr, 0)
            if pr > 0:
                attn_pv(qc, 2 * pr - 1, hidx + 2 * pr - 1)
            if not (qc == 0 and pr == 0):
                attn_st_pair(qc, 2 * pr, hidx + 2 * pr, 1)
            attn_pv(qc, 2 * pr, hidx + 2 * pr)
            if qc + 1 < nqc and 2 <= pr <= 5:
                ln1_stats(qc + 1, (pr - 2,))
            if qc + 1 < nqc and 4 <= pr <= 7:
                ln1_apply(qc + 1, (pr - 4,))
            if qc + 1 < nqc and 5 <= pr <= 7:
                ln1_rest(qc + 1, (pr - 5,), applied=True)
        attn_pv(qc, H - 1, hidx + H - 1)
        if qc + 1 < nqc:
            ln1_rest(qc + 1, (3,), applied=True)
        hidx += H
        def run_pairs(nxt):
            wo_mm(qc, 0)
            if len(nxt) > 0:
                nxt[0]()
            wo_mm(qc, 1)
            wo_tail_stats(qc, 0)
            if len(nxt) > 1:
                nxt[1]()
            wo_tail_rest(qc, 0)
            if len(nxt) > 2:
                nxt[2]()
            wo_mm(qc, 2)
            wo_mm(qc, 3)
            wo_tail_stats(qc, 2)
            wo_tail_rest(qc, 2)

        if qc + 1 < nqc:
            run_pairs([lambda q=qc + 1: proj_chunk(q, "q"),
                       lambda q=qc + 1: proj_chunk(q, "k"),
                       lambda q=qc + 1: proj_chunk(q, "v")])
        else:
            for nm in ("x1r0", "x1r1"):
                for ev in haz([res("x0g0")]):
                    res(nm).note_read(ev)
            for ev in haz([res("astg0"), res("astg1")]):
                res("fstg0").note_read(ev)
                res("fstg1").note_read(ev)
            run_pairs([lambda: ffn_prep(0),
                       lambda: ffn_prep_stats(1)])
        if qc + 2 < nqc:
            x0_dma(qc + 2)
        sch.mark(f"qc{qc}_end")

    rW1a, rW1b, rW2a, rW2b, rHS = res("w1sba"), res("w1sbb"), res("w2sba"), res("w2sbb"), res("HS")
    w1a_deps = haz([res(n) for n in ("wqt", "wkt", "wvt")])
    w1b_deps = haz([res(f"aint{s}") for s in range(8)])
    w2a_deps = haz([res("qT"), res("kts0"), res("kts1")])
    w2b_deps = haz([res(f"vx{s}") for s in range(8)]
                   + [res(f"pt{p}g{g}") for p in range(3) for g in range(2)])
    w2c_deps = haz([res("wot")])
    dma("gpsimd", ab(oW1, 24576).rearrange("p (db f) -> p db f", f=FF),
        dr["w1"][0 : 6 * P, :].rearrange("(db p) f -> p db f", p=P),
        deps=w1a_deps, writes=[rW1a])
    dma("gpsimd", ab(oW1B, 8192).rearrange("p (db f) -> p db f", f=FF),
        dr["w1"][6 * P : 8 * P, :].rearrange("(db p) f -> p db f", p=P),
        deps=w1b_deps, writes=[rW1b])
    dma("gpsimd", ab(oW2A, 12288).rearrange("p (fb c) -> p fb c", c=C),
        dr["w2"][0 : 12 * P, :].rearrange("(fb p) c -> p fb c", p=P),
        deps=w2a_deps, writes=[rW2a])
    dma("gpsimd", ab(oW2B, 14336).rearrange("p (fb c) -> p fb c", c=C),
        dr["w2"][12 * P : 26 * P, :].rearrange("(fb p) c -> p fb c", p=P),
        deps=w2b_deps, writes=[rW2a])
    dma("gpsimd", ab(oW2C, 6144).rearrange("p (fb c) -> p fb c", c=C),
        dr["w2"][26 * P : 32 * P, :].rearrange("(fb p) c -> p fb c", p=P),
        deps=w2c_deps, writes=[rW2b])
    hs_deps = haz([res("pt2g0"), res("pt2g1")]
                  + [res(f"an{i}") for i in range(4)] + [res("at0"), res("at1")])
    for ev in hs_deps:
        rHS.note_read(ev)
    for ev in haz([res("wot")]):
        res("ft").note_read(ev)
    x0_deps = haz([res("x0g1")])
    for nm in ("x1r2", "yt0", "yt1"):
        for ev in x0_deps:
            res(nm).note_read(ev)
    for nm in ("x2st0", "x2st1"):
        for ev in haz([res("x1st0"), res("x1st1")]):
            res(nm).note_read(ev)
    gbb_deps = haz([res("x1st2"), res("x1st3")])
    ffn_reload(1)
    dma("sync", af(oGB, 1024), dr["gb"], deps=gbb_deps, writes=[res("gbs")], sem_name="qX")
    dma("sync", af(oBB, 1024), dr["bb"], deps=gbb_deps, writes=[res("bbs")], sem_name="qX")
    dma("gpsimd", sb["b2rb"][:], dr["b2r"], writes=[res("b2s")])
    dma("sync", sb["onesb"][:], dr["onesb"], writes=[res("ones")], sem_name="qC")

    sch.mark("weights_done")
    ffn_prep_trans(0)
    for k in range(nch):
        if 1 <= k + 1 < nch:
            ffn_prep(k + 1)
        ffn_w1(k)
        if k > 0:
            ffn_epi_rest(k - 1)
        ffn_w2(k, 0)
        if k + 1 < nch:
            ffn_prep_trans(k + 1)
        ffn_epi_eo(k, 0)
        if k == nch - 1:
            ffn_epi_rest(k, us=(0,))
        ffn_w2(k, 1)
        ffn_epi_eo(k, 1)
        sch.mark(f"ffn{k}_end")
    ffn_epi_rest(nch - 1, us=(1,))

    sch.streams["sync"].append(([("qS", sch.counts["qS"]), ("qX", sch.counts["qX"])], None, None, 1))
    sch.streams["gpsimd"].append(([("qG", sch.counts["qG"])], None, None, 1))


def build_nc(nt=NT):
    nc = bass.Bass("TRN2", target_bir_lowering=False)
    dr = {}
    dr["xs"] = nc.dram_tensor("xs", [HALO + nt, C], F32, kind="ExternalInput").ap()
    for w in ("wqt", "wkt", "wvt", "wot"):
        dr[w] = nc.dram_tensor(w, [C, C], F32, kind="ExternalInput").ap()
    dr["w1"] = nc.dram_tensor("w1", [C, FF], F32, kind="ExternalInput").ap()
    dr["w2"] = nc.dram_tensor("w2", [FF, C], F32, kind="ExternalInput").ap()
    dr["b1r"] = nc.dram_tensor("b1r", [P, FF // P], F32, kind="ExternalInput").ap()
    dr["b2r"] = nc.dram_tensor("b2r", [1, C], F32, kind="ExternalInput").ap()
    dr["onesb"] = nc.dram_tensor("onesb", [1, P], BF16, kind="ExternalInput").ap()
    dr["gb"] = nc.dram_tensor("gb", [P, C], F32, kind="ExternalInput").ap()
    dr["bb"] = nc.dram_tensor("bb", [P, C], F32, kind="ExternalInput").ap()
    dr["mm"] = nc.dram_tensor("mm", [P, 256], BF16, kind="ExternalInput").ap()
    dr["mh"] = nc.dram_tensor("mh", [P, 512], BF16, kind="ExternalInput").ap()
    dr["identb"] = nc.dram_tensor("identb", [P, P], BF16, kind="ExternalInput").ap()
    dr["cst"] = nc.dram_tensor("cst", [P, 8], F32, kind="ExternalInput").ap()
    dr["y"] = nc.dram_tensor("y", [nt, C], F32, kind="ExternalOutput").ap()
    dr["x1s"] = nc.dram_tensor("x1s", [nt, C], F32, kind="Internal").ap()

    with contextlib.ExitStack() as ctx:
        E = ctx.enter_context
        sb = {}
        sb["cst"] = E(nc.sbuf_tensor("cst_s", [P, 8], F32))
        sb["identb"] = E(nc.sbuf_tensor("identb_s", [P, P], BF16))
        sb["mm"] = E(nc.sbuf_tensor("mm_s", [P, 256], BF16))
        sb["mh"] = E(nc.sbuf_tensor("mh_s", [P, 512], BF16))
        sb["b1r"] = E(nc.sbuf_tensor("b1r_s", [P, FF // P], F32))
        sb["b2rb"] = E(nc.sbuf_tensor("b2rb_s", [1, C], BF16))
        sb["onesb"] = E(nc.sbuf_tensor("onesb_s", [1, P], BF16))
        sb["stat"] = E(nc.sbuf_tensor("stat_s", [P, 128], F32))
        sb["rn"] = E(nc.sbuf_tensor("rn_s", [P, 16], F32))
        sb["rcp"] = E(nc.sbuf_tensor("rcp_s", [P, 4], F32))
        sb["AB"] = E(nc.sbuf_tensor("arena_b", [P, NB], BF16))
        sb["AF"] = E(nc.sbuf_tensor("arena_f", [P, NF], F32))
        sb["PSGA"] = E(nc.psum_tensor("psga", [P, 1536], F32))
        sb["PSGB"] = E(nc.psum_tensor("psgb", [P, 1536], F32))
        sb["PSM"] = E(nc.psum_tensor("psm", [P, 1024], F32))
        sems = {}
        for s in ("qS", "qG", "qX", "qC", "qR", "gG", "tT", "vV", "sS"):
            sems[s] = E(nc.semaphore(s))
        block = E(nc.Block())
        for i, cv in enumerate(CONSTS):
            nc.const_aps.register(F32, cv, sb["cst"][:, i : i + 1])

        sch = Sched()
        schedule(nc, sch, dr, sb, nt=nt)

        def runner(eng_name):
            def run(eng):
                for wl, emit, sem, by in sch.streams[eng_name]:
                    for s, v in wl:
                        eng.wait_ge(sems[s], v)
                    if emit is None:
                        continue
                    inst = emit(eng)
                    if sem is not None:
                        inst.then_inc(sems[sem], by)
            return run

        block.sync(runner("sync"))
        block.gpsimd(runner("gpsimd"))
        block.tensor(runner("tensor"))
        block.vector(runner("vector"))
        block.scalar(runner("scalar"))

    return nc


_NC_CACHE = {}


def _get_nc():
    if "nc" not in _NC_CACHE:
        _NC_CACHE["nc"] = build_nc()
    return _NC_CACHE["nc"]


def _make_masks():
    kk = np.arange(P)[:, None]
    u = np.arange(1536)[None, :]
    Mfull = np.where((u >= kk + 512) & (u <= kk + 1023), 0.0, NEG).astype(np.float32)
    mm = np.concatenate([Mfull[:, 512:640], Mfull[:, 1024:1152]], axis=1)
    mh1 = Mfull[:, 640:1152]
    mh0 = np.full((P, 512), NEG, np.float32)
    return (np.ascontiguousarray(mm).astype(BF), np.ascontiguousarray(mh1).astype(BF),
            mh0.astype(BF))


def kernel(L0_attn_x, L0_attn_Wq, L0_attn_Wk, L0_attn_Wv, L0_attn_Wo,
           L0_ffn_router_w, L0_ffn_router_b, L0_ffn_w1, L0_ffn_b1, L0_ffn_w2,
           L0_ffn_b2, L0_ffn_gamma, L0_ffn_beta):
    x = np.ascontiguousarray(np.asarray(L0_attn_x, dtype=np.float32))
    wqt = np.ascontiguousarray(np.asarray(L0_attn_Wq, np.float32).T)
    wkt = np.ascontiguousarray(np.asarray(L0_attn_Wk, np.float32).T)
    wvt = np.ascontiguousarray(np.asarray(L0_attn_Wv, np.float32).T)
    wot = np.ascontiguousarray(np.asarray(L0_attn_Wo, np.float32).T)
    w1 = np.ascontiguousarray(np.asarray(L0_ffn_w1, np.float32)[0])
    b1 = np.asarray(L0_ffn_b1, np.float32)[0]
    w2 = np.ascontiguousarray(np.asarray(L0_ffn_w2, np.float32)[0])
    b2 = np.asarray(L0_ffn_b2, np.float32)[0]
    gamma = np.asarray(L0_ffn_gamma, np.float32)
    beta = np.asarray(L0_ffn_beta, np.float32)

    b1r = np.ascontiguousarray(b1.reshape(FF // P, P).T)
    b2r = np.ascontiguousarray(b2.reshape(1, C))
    onesb = np.ones((1, P), dtype=BF)
    gb = np.ascontiguousarray(np.broadcast_to(gamma, (P, C)))
    bb = np.ascontiguousarray(np.broadcast_to(beta, (P, C)))
    mm, mh1, mh0 = _make_masks()
    identb = np.eye(P, dtype=BF)
    cst = np.ascontiguousarray(
        np.broadcast_to(np.array(CONSTS + [0.0] * (8 - len(CONSTS)), np.float32), (P, 8)))

    nc = _get_nc()
    in_maps = []
    for core in range(8):
        b, half = core // 2, core % 2
        t0 = half * NT
        if half == 0:
            xs = np.concatenate([np.zeros((HALO, C), np.float32), x[b, 0:NT]], axis=0)
            mh = mh0
        else:
            xs = np.ascontiguousarray(x[b, t0 - HALO : t0 + NT])
            mh = mh1
        in_maps.append(dict(xs=xs, wqt=wqt, wkt=wkt, wvt=wvt, wot=wot, w1=w1,
                            w2=w2, b1r=b1r, b2r=b2r, onesb=onesb, gb=gb, bb=bb,
                            mm=mm, mh=mh, identb=identb, cst=cst))
    res = run_bass_kernel_spmd(nc, in_maps, core_ids=list(range(8)))
    out = np.empty((4, 4096, C), np.float32)
    for core in range(8):
        b, half = core // 2, core % 2
        out[b, half * NT : (half + 1) * NT] = res.results[core]["y"]
    return out







